# version 01 (final)
# baseline (speedup 1.0000x reference)
import numpy as np
import concourse.bass as bass
import concourse.mybir as mybir
from concourse.bass_utils import run_bass_kernel_spmd

F32 = mybir.dt.float32
BF16 = mybir.dt.bfloat16
AF = mybir.ActivationFunctionType
ALU = mybir.AluOpType

S, D, I, N, H, KVH, HD, DTR, KC = 4096, 1024, 2048, 16, 16, 4, 64, 64, 4
L = 512
NCH_FULL = S // L
TSUB = L // 128
NIC = I // 128
DC = D // 128
LN_EPS = 1e-5
RMS_EPS = 1.1920929e-07
ENGINES = ("tensor", "vector", "scalar", "gpsimd", "sync")

N_CHUNKS = NCH_FULL
DEBUG_TAPS = []
PHASE_LIMIT = 10 ** 9
SV_MODE = 0


class KB:
    def __init__(self, nc):
        self.nc = nc
        self.sems = {}
        self.cnt = {}
        self.bar = nc.alloc_semaphore(name="bar")
        self.bar_expect = 0

    def sem(self, name):
        if name not in self.sems:
            self.sems[name] = self.nc.alloc_semaphore(name=name)
            self.cnt[name] = 0
        return self.sems[name]

    def inc(self, instr, name, k=1):
        instr.then_inc(self.sem(name), k)
        self.cnt[name] += k
        return self.cnt[name]

    def snap(self, *names):
        return {n: self.cnt.get(n, 0) for n in names}

    def phase(self, **emitters):
        self.nphase = getattr(self, "nphase", 0) + 1
        if self.nphase > PHASE_LIMIT:
            return
        with self.nc.Block() as b:
            for ename, fn in emitters.items():
                dec = getattr(b, ename)

                def wrap(e, fn=fn, ename=ename):
                    last = fn(e)
                    if last is not None:
                        if ename != "sync":
                            e.drain()
                        e.sem_inc(self.bar, 1)
                        self.bar_expect += 1
                dec(wrap)

    def barrier(self):
        if getattr(self, "nphase", 0) > PHASE_LIMIT:
            return
        tgt = self.bar_expect
        with self.nc.Block() as b:
            for ename in ENGINES:
                dec = getattr(b, ename)

                def wrap(e, tgt=tgt):
                    e.wait_ge(self.bar, tgt)
                dec(wrap)


def build_program():
    nc = bass.Bass("TRN2", target_bir_lowering=False)
    kb = KB(nc)
    C = kb.cnt
    sem = kb.sem

    def din(name, shape):
        return nc.dram_tensor(name, list(shape), F32, kind="ExternalInput").ap()

    x_h = din("x", (S, D))
    w_mamba = din("W_mamba", (2 * I, D))
    w_q = din("W_q", (D, D))
    w_kv = din("W_kv", (2 * KVH * HD, D))
    q_gain = din("q_gain", (H,))
    conv_w = din("conv_w", (I, KC))
    conv_b = din("conv_b", (I,))
    w_xproj = din("W_xproj", (DTR + 2 * N, I))
    w_dt = din("W_dt", (I, DTR))
    b_dt = din("b_dt", (I,))
    a_log = din("A_log", (I, N))
    d_in = din("D", (I,))
    w_mo = din("W_mamba_out", (D, I))
    w_proj = din("W_proj", (D, D))
    m_alpha = din("merge_alpha", (1,))
    wf_r = din("W_r", (D, D))
    wf_k = din("W_k", (D, D))
    wf_v = din("W_v", (D, D))
    mu_r = din("mu_r", (D,))
    mu_k = din("mu_k", (D,))
    out_h = nc.dram_tensor("out", [S, D], F32, kind="ExternalOutput").ap()

    wc_q = nc.dram_tensor("wc_q", [DC, 128, 1024], BF16).ap()
    wc_kv = nc.dram_tensor("wc_kv", [DC, 128, 512], BF16).ap()
    wc_mamba = nc.dram_tensor("wc_mamba", [2 * NIC, DC, 128, 128], BF16).ap()
    wc_mo = nc.dram_tensor("wc_mo", [DC, NIC, 128, 128], BF16).ap()
    wc_proj = nc.dram_tensor("wc_proj", [DC, 128, 1024], BF16).ap()
    wc_fr = nc.dram_tensor("wc_fr", [DC, 128, 1024], BF16).ap()
    wc_fk = nc.dram_tensor("wc_fk", [DC, DC, 128, 128], BF16).ap()
    wc_fv = nc.dram_tensor("wc_fv", [DC, 128, 1024], BF16).ap()
    den_hbm = nc.dram_tensor("den_hbm", [1, L], F32).ap()

    sbuf = {}

    def sb(n, s, d):
        ap = nc.alloc_sbuf_tensor(n, list(s), d).ap()
        sbuf[n] = ap
        return ap

    xn_fm = sb("xn_fm", (128, DC * (L + 1)), BF16)
    kglob = sb("kglob", (128, 2 * S), BF16)
    vglob = sb("vglob", (128, KVH * (S // 128) * 65), BF16)
    qhat = sb("qhat", (128, DC * L), BF16)
    attn_s = sb("attn_s", (128, DC * L), BF16)
    u_g = sb("u_g", (128, NIC * L), BF16)
    gate_bf = sb("gate_bf", (128, NIC * L), BF16)
    y_gated = sb("y_gated", (128, NIC * L), BF16)
    bbc = sb("bbc", (128, N * L), BF16)
    cbc = sb("cbc", (128, N * L), BF16)
    dtlow = sb("dtlow", (128, L), BF16)
    state = sb("state", (128, NIC * N), F32)
    x_tm = sb("x_tm", (128, TSUB * D), F32)
    ktm = sb("ktm", (128, D), BF16)
    cos_tm = sb("cos_tm", (128, 32 * (S // 128)), BF16)
    sin_tm = sb("sin_tm", (128, 32 * (S // 128)), BF16)
    ident = sb("ident", (128, 128), BF16)
    onesr32 = sb("onesr32", (1, 128), F32)
    onesbf = sb("onesbf", (1, 128), BF16)
    ramp = sb("ramp", (128, 896), BF16)
    iota_p = sb("iota_p", (128, 1), F32)
    wx_T = sb("wx_T", (128, NIC * 96), BF16)
    wdt_T = sb("wdt_T", (128, (NIC // 2) * 128), BF16)
    acols = sb("acols", (128, NIC * N), F32)
    svcols = sb("svcols", (128, 64), F32)
    dcols = svcols[:, 0:16]
    bdtcols = svcols[:, 48:64]
    cwcols = sb("cwcols", (128, NIC * KC), F32)
    cbcols = svcols[:, 16:32]
    murcols = svcols[:, 32:40]
    mukcols = svcols[:, 40:48]
    qscale = sb("qscale", (128, H), F32)
    qbias = sb("qbias", (128, H), F32)
    alcols = sb("alcols", (128, 2), F32)
    convhalo = sb("convhalo", (128, NIC * 3), F32)
    wring = sb("wring", (128, 8 * 128), BF16)
    scr0 = sb("scr0", (128, 64), F32)
    scr2 = sb("scr2", (128, L), F32)
    scr3 = sb("scr3", (128, L), F32)
    scrT = sb("scrT", (128, L), BF16)
    scrB = sb("scrB", (32, L), BF16)
    scrBF = sb("scrBF", (1, 2 * L), BF16)
    scrK = sb("scrK", (128, L), BF16)
    invb = sb("invb", (128, 32), F32)
    wringB = sb("wringB", (128, 2 * 1024), BF16)
    wringC = sb("wringC", (128, 2 * 512), BF16)
    ropet = sb("ropet", (128, 1024), BF16)
    delta_sb = sb("delta_sb", (128, L), F32)
    du_bf = sb("du_bf", (128, L), BF16)
    sg_sb = sb("sg_sb", (128, L), F32)
    expS = sb("expS", (128, 4 * L), BF16)
    pn = sb("pn", (128, 2 * L), BF16)
    abuf = sb("abuf", (128, 2 * L), BF16)
    bbuf = sb("bbuf", (128, 2 * L), BF16)
    hbuf = sb("hbuf", (128, L), F32)

    den2A = scr3[64:65, :]
    den2B = scr2[64:65, :]
    rbcA = scr3[0:64, :]
    rbcB = scr2[0:64, :]
    sum_fm = scr2
    merged = qhat
    ffn_fm = u_g.bitcast(F32)
    r_bf = y_gated
    kact = attn_s
    xr_g = bbc
    xk_g = cbc

    psA = nc.alloc_psum_tensor("psA", [128, L], F32).ap()
    psB = nc.alloc_psum_tensor("psB", [128, L], F32).ap()
    psC = nc.alloc_psum_tensor("psC", [128, L], F32).ap()
    psD = nc.alloc_psum_tensor("psD", [128, L], F32).ap()
    psE = nc.alloc_psum_tensor("psE", [128, 1024], F32).ap()
    psT = nc.alloc_psum_tensor("psT", [128, L], F32).ap()
    psR = psE[0:2, 0:L]
    psY0 = psE[0:65, 0:L]
    psY1 = psE[0:65, L:2 * L]

    def tslot(j):
        return [psA, psB, psC, psD][j % 4][:, 0:128]

    assert nc.sbuf_bytes_remaining >= 0, f"SBUF over by {-nc.sbuf_bytes_remaining}"

    dbg_out = {}
    for tag, name, rows, cols in DEBUG_TAPS:
        dbg_out[(tag)] = (nc.dram_tensor("dbg_" + tag, [rows, cols], F32,
                                         kind="ExternalOutput").ap(), name, rows, cols)

    def tap(tag):
        if tag not in dbg_out:
            return
        dst, name, rows, cols = dbg_out[tag]

        def em(e):
            instr = e.dma_start(out=dst, in_=sbuf[name][0:rows, 0:cols])
            kb.inc(instr, "d_dbg", 16)
            return None
        kb.phase(gpsimd=em)
        kb.barrier()

    def pr_gp(e):
        e.memset(x_tm[:], 0.0)
        e.memset(u_g[:], 0.0)
        e.iota(iota_p[:], [[1, 1]], base=0, channel_multiplier=1,
               allow_small_or_imprecise_dtypes=True)
        e.iota(scr2[:, 0:128], [[1, 128]], base=0, channel_multiplier=0,
               allow_small_or_imprecise_dtypes=True)
        e.iota(x_tm[:, 0:896], [[1, 896]], base=0, channel_multiplier=0,
               allow_small_or_imprecise_dtypes=True)
        e.memset(onesr32[:], 1.0)
        e.memset(onesbf[:], 1.0)
        e.memset(vglob[:].rearrange("p (k c) -> p k c", c=65)[:, :, 64], 1.0)
        return e.memset(state[:], 0.0)

    def pr_ve(e):
        return e.memset(convhalo[:], 0.0)

    kb.phase(gpsimd=pr_gp, vector=pr_ve)
    kb.barrier()

    def pr1_ve(e):
        e.tensor_scalar(out=ident[:], in0=scr2[:, 0:128], scalar1=iota_p[:, 0:1],
                        scalar2=None, op0=ALU.is_equal)
        last = e.tensor_scalar(out=ramp[:], in0=x_tm[:, 0:896],
                               scalar1=iota_p[:, 0:1], scalar2=384.0,
                               op0=ALU.subtract, op1=ALU.is_ge)
        for dc in range(DC):
            last = e.memset(xn_fm[:, dc * (L + 1):dc * (L + 1) + 1], 0.0)
        return last

    kb.phase(vector=pr1_ve)
    kb.barrier()

    def pr2_sync(e):
        ops = [
            (acols[:].rearrange("p (ic n) -> p ic n", n=N),
             a_log.rearrange("(ic p) n -> p ic n", p=128)),
            (cwcols[:].rearrange("p (ic k) -> p ic k", k=KC),
             conv_w.rearrange("(ic p) k -> p ic k", p=128)),
            (scr0[:, 0:H], bass.AP(q_gain.tensor, 0, [[0, 128], [1, H]])),
            (scr0[:, 16:17], bass.AP(m_alpha.tensor, 0, [[0, 128], [1, 1]])),
            (x_tm[0:96, 1152:1152 + I], w_xproj[:, :]),
            (x_tm[0:16, 0:128], d_in.rearrange("(ic p) -> ic p", p=128)),
            (x_tm[16:32, 0:128], conv_b.rearrange("(ic p) -> ic p", p=128)),
            (x_tm[32:40, 0:128], mu_r.rearrange("(c p) -> c p", p=128)),
            (x_tm[40:48, 0:128], mu_k.rearrange("(c p) -> c p", p=128)),
            (x_tm[48:64, 0:128], b_dt.rearrange("(ic p) -> ic p", p=128)),
            (x_tm[:, 128:128 + NIC * 64].rearrange("p (ic m) -> p ic m", m=64),
             w_dt.rearrange("(ic p) m -> p ic m", p=128)),
        ]
        for o, i_ in ops:
            instr = e.dma_start(out=o, in_=i_)
            kb.inc(instr, "d_pro", 16)
        return None

    kb.phase(sync=pr2_sync)

    def pr3_ve(e):
        e.wait_ge(sem("d_pro"), C["d_pro"])
        e.tensor_scalar(out=qscale[:], in0=scr0[:, 0:H], scalar1=0.125,
                        scalar2=None, op0=ALU.mult)
        e.tensor_scalar(out=qbias[:], in0=scr0[:, 0:H], scalar1=-1.0,
                        scalar2=None, op0=ALU.mult)
        e.drain()
        e.tensor_tensor(out=qbias[:], in0=qbias[:], in1=scr0[:, 0:H], op=ALU.max)
        e.drain()
        last = e.tensor_scalar(out=qbias[:], in0=qbias[:], scalar1=-8.0,
                               scalar2=None, op0=ALU.mult)
        kb.inc(last, "ve", 1)
        return last

    def pr3_sc(e):
        e.wait_ge(sem("d_pro"), C["d_pro"])
        e.activation(out=acols[:], in_=acols[:], func=AF.Exp)
        last = e.activation(out=alcols[:, 0:1], in_=scr0[:, 16:17], func=AF.Sigmoid)
        kb.inc(last, "sc", 1)
        return last

    kb.phase(vector=pr3_ve, scalar=pr3_sc)

    def pr4_ve(e):
        e.wait_ge(sem("sc"), C["sc"])
        e.tensor_scalar(out=acols[:], in0=acols[:], scalar1=-1.0,
                        scalar2=None, op0=ALU.mult)
        return e.tensor_scalar(out=alcols[:, 1:2], in0=alcols[:, 0:1], scalar1=-1.0,
                               scalar2=1.0, op0=ALU.mult, op1=ALU.add)

    kb.phase(vector=pr4_ve)
    kb.barrier()

    def sv_te(e):
        if SV_MODE == 1:
            return None
        e.wait_ge(sem("ve"), sv_base["ve"] + 1)
        ve2_base = sv_base["ve2"]
        last = None
        _svn = {2: 4, 4: 8, 5: 16, 6: 4}.get(SV_MODE, NIC // 2 + NIC)
        for j in range(_svn):
            if j >= 4 and SV_MODE not in (2, 3):
                e.wait_ge(sem("ve2"), ve2_base + j - 3)
            src_t = (u_g[:, j * 128:(j + 1) * 128] if j < NIC // 2 else
                     u_g[:, 2048 + (j - NIC // 2) * 128:2048 + (j - NIC // 2 + 1) * 128])
            last = e.matmul(tslot(j), lhsT=src_t, rhs=ident[:], start=True, stop=True)
            kb.inc(last, "mm", 1)
        return last

    def sv_ve(e):
        base_mm = sv_base["mm"]
        if SV_MODE == 2:
            e.sem_inc(sem("ve"), 1); C["ve"] += 1
            return e.engine_nop()
        for I in range(2):
            for J in range(4):
                e.transpose(out=svcols[32 * J:32 * J + 32, 32 * I:32 * I + 32],
                            in_=x_tm[32 * I:32 * I + 32, 32 * J:32 * J + 32])
        e.tensor_copy(out=u_g[:, 0:NIC * 64], in_=x_tm[:, 128:128 + NIC * 64])
        last = e.tensor_copy(out=u_g[:, 2048:2048 + 2048], in_=x_tm[:, 1152:1152 + 2048])
        kb.inc(last, "ve", 1)
        _svn2 = {1: 0, 3: 0, 4: 8, 5: 16, 6: 4}.get(SV_MODE, NIC // 2 + NIC)
        for j in range(_svn2):
            e.wait_ge(sem("mm"), base_mm + 1 + j)
            sl = tslot(j)
            if j < NIC // 2:
                e.tensor_copy(out=wdt_T[0:64, j * 128:(j + 1) * 128], in_=sl[0:64, :])
                last = e.tensor_copy(out=wdt_T[64:128, j * 128:(j + 1) * 128],
                                     in_=sl[64:128, :])
            else:
                ic = j - NIC // 2
                last = e.tensor_copy(out=wx_T[:, ic * 96:(ic + 1) * 96], in_=sl[:, 0:96])
            kb.inc(last, "ve2", 1)
        return last

    sv_base = kb.snap("mm", "ve", "ve2")
    kb.phase(vector=sv_ve, tensor=sv_te)
    kb.barrier()
    tap("svcols")
    tap("qscale")
    tap("acols")
    tap("wdt_T")

    def rt_gp(e):
        return e.iota(invb[:], [[1, 32]], base=0, channel_multiplier=0,
                      allow_small_or_imprecise_dtypes=True)

    kb.phase(gpsimd=rt_gp)

    def rt_sc(e):
        lnb = float(np.log(10000.0) / 32.0)
        last = e.activation(out=invb[:], in_=invb[:], func=AF.Exp, scale=-lnb)
        kb.inc(last, "sc", 1)
        return last

    kb.phase(scalar=rt_sc)
    kb.barrier()
    tap("invb")

    inv2pi = float(1.0 / (2 * np.pi))
    twopi = float(2 * np.pi)
    pi = float(np.pi)
    for tb in range(S // 128):
        def rtb_ve(e, tb=tb):
            e.wait_ge(sem("sc"), C["sc"])
            e.tensor_scalar(out=scr0[:, 36:37], in0=iota_p[:], scalar1=float(tb * 128),
                            scalar2=inv2pi, op0=ALU.add, op1=ALU.mult)
            e.drain()
            e.tensor_scalar(out=scr2[:, 0:32], in0=invb[:], scalar1=scr0[:, 36:37],
                            scalar2=None, op0=ALU.mult)
            e.drain()
            e.tensor_scalar(out=scr2[:, 64:96], in0=scr2[:, 0:32], scalar1=8388608.0,
                            scalar2=-8388608.0, op0=ALU.add, op1=ALU.add)
            e.drain()
            e.tensor_tensor(out=scr2[:, 96:128], in0=scr2[:, 0:32], in1=scr2[:, 64:96],
                            op=ALU.subtract)
            e.drain()
            e.tensor_scalar(out=scr2[:, 128:160], in0=scr2[:, 96:128], scalar1=0.0,
                            scalar2=None, op0=ALU.is_lt)
            e.drain()
            e.tensor_tensor(out=scr2[:, 96:128], in0=scr2[:, 96:128], in1=scr2[:, 128:160],
                            op=ALU.add)
            e.drain()
            e.tensor_scalar(out=scr2[:, 160:192], in0=scr2[:, 96:128], scalar1=-0.5,
                            scalar2=twopi, op0=ALU.add, op1=ALU.mult)
            e.drain()
            e.tensor_scalar(out=scr2[:, 192:224], in0=scr2[:, 160:192], scalar1=pi / 2,
                            scalar2=None, op0=ALU.add)
            e.drain()
            e.tensor_scalar(out=scr2[:, 224:256], in0=scr2[:, 192:224], scalar1=pi,
                            scalar2=twopi, op0=ALU.is_gt, op1=ALU.mult)
            e.drain()
            last = e.tensor_tensor(out=scr2[:, 192:224], in0=scr2[:, 192:224],
                                   in1=scr2[:, 224:256], op=ALU.subtract)
            kb.inc(last, "ve", 1)
            return last

        def rtb_sc(e, tb=tb):
            e.wait_ge(sem("ve"), C["ve"])
            e.activation(out=scr3[:, 0:32], in_=scr2[:, 160:192], func=AF.Sin)
            last = e.activation(out=scr3[:, 32:64], in_=scr2[:, 192:224], func=AF.Sin)
            kb.inc(last, "sc", 1)
            return last

        kb.phase(vector=rtb_ve, scalar=rtb_sc)

        def rtb2_ve(e, tb=tb):
            e.wait_ge(sem("sc"), C["sc"])
            e.tensor_scalar(out=sin_tm[:, tb * 32:(tb + 1) * 32], in0=scr3[:, 0:32],
                            scalar1=-1.0, scalar2=None, op0=ALU.mult)
            return e.tensor_scalar(out=cos_tm[:, tb * 32:(tb + 1) * 32], in0=scr3[:, 32:64],
                                   scalar1=-1.0, scalar2=None, op0=ALU.mult)

        kb.phase(vector=rtb2_ve)
    kb.barrier()
    tap("rope_cos")
    tap("rope_sin")

    wtmp32 = x_tm
    wtmp16 = u_g[:, 0:4096]
    wstg = gate_bf[:, 0:2048]

    def cache_weight(wap, rows, cols, dst_fn):
        RB, CB = rows // 128, cols // 128
        for rb in range(RB):
            sl = rb % 2

            def w_sync(e, rb=rb, sl=sl):
                instr = e.dma_start(out=wtmp32[:, sl * 2048:sl * 2048 + cols],
                                    in_=wap[rb * 128:(rb + 1) * 128, :])
                kb.inc(instr, "d_w", 16)
                return None

            kb.phase(sync=w_sync)

            def w_ve(e, rb=rb, sl=sl):
                e.wait_ge(sem("d_w"), C["d_w"])
                last = e.tensor_copy(out=wtmp16[:, sl * 2048:sl * 2048 + cols],
                                     in_=wtmp32[:, sl * 2048:sl * 2048 + cols])
                kb.inc(last, "ve", 1)
                return last

            kb.phase(vector=w_ve)

            def w_te(e, rb=rb, sl=sl, CB=CB):
                e.wait_ge(sem("ve"), C["ve"])
                ev_base = C.get("ve2g", 0)
                last = None
                for cb in range(CB):
                    g = cb // 4
                    if g >= 1 and cb % 4 == 0:
                        e.wait_ge(sem("ve2g"), ev_base + g)
                    last = e.matmul(tslot(cb),
                                    lhsT=wtmp16[:, sl * 2048 + cb * 128:sl * 2048 + (cb + 1) * 128],
                                    rhs=ident[:], start=True, stop=True)
                    if cb % 4 == 3 or cb == CB - 1:
                        kb.inc(last, "mm", 1)
                return last

            def w_ev(e, rb=rb, CB=CB, base=None):
                ngroups = (CB + 3) // 4
                base = C["mm"] - ngroups
                last = None
                for g in range(ngroups):
                    e.wait_ge(sem("mm"), base + g + 1)
                    for qq in range(4):
                        cb = g * 4 + qq
                        if cb >= CB:
                            break
                        last = e.tensor_copy(out=wstg[:, cb * 128:(cb + 1) * 128],
                                             in_=tslot(cb))
                    kb.inc(last, "ve2g", 1)
                e.sem_inc(sem("ve2"), 1)
                C["ve2"] += 1
                return last

            kb.phase(tensor=w_te, vector=w_ev)

            def w_st(e, rb=rb, CB=CB):
                e.wait_ge(sem("ve2"), C["ve2"])
                for cb in range(CB):
                    instr = e.dma_start(out=dst_fn(rb, cb), in_=wstg[:, cb * 128:(cb + 1) * 128])
                    kb.inc(instr, "d_wo", 16)
                e.wait_ge(sem("d_wo"), C["d_wo"])
                return None

            kb.phase(sync=w_st)
            kb.barrier()

    cache_weight(w_q, D, D, lambda rb, cb: wc_q[cb, :, rb * 128:(rb + 1) * 128])
    cache_weight(w_kv, 512, D, lambda rb, cb: wc_kv[cb, :, rb * 128:(rb + 1) * 128])
    cache_weight(w_mamba, 2 * I, D, lambda rb, cb: wc_mamba[rb, cb, :, :])
    cache_weight(w_mo, D, I, lambda rb, cb: wc_mo[rb, cb, :, :])
    cache_weight(w_proj, D, D, lambda rb, cb: wc_proj[cb, :, rb * 128:(rb + 1) * 128])
    cache_weight(wf_r, D, D, lambda rb, cb: wc_fr[cb, :, rb * 128:(rb + 1) * 128])
    cache_weight(wf_k, D, D, lambda rb, cb: wc_fk[rb, cb, :, :])
    cache_weight(wf_v, D, D, lambda rb, cb: wc_fv[cb, :, rb * 128:(rb + 1) * 128])

    ring = {"n": 0}
    pend = []

    def wload(e, src_ap):
        s = ring["n"] % 8
        if ring["n"] >= 8:
            e.wait_ge(sem("wuse"), ring["n"] - 8 + 1)
        instr = e.dma_start(out=wring[:, s * 128:(s + 1) * 128], in_=src_ap)
        tgt = kb.inc(instr, f"d_wA{s}", 16)
        ring["n"] += 1
        pend.append((s, tgt))

    def wmm(e, out, rhs, start, stop):
        s, tgt = pend.pop(0)
        e.wait_ge(sem(f"d_wA{s}"), tgt)
        instr = e.matmul(out, lhsT=wring[:, s * 128:(s + 1) * 128], rhs=rhs,
                         start=start, stop=stop)
        kb.inc(instr, "wuse", 1)
        return instr

    sem("wuse")
    sem("d_out")

    def xn_ck(ck):
        return xn_fm[:, ck * (L + 1) + 1:(ck + 1) * (L + 1)]

    for c in range(N_CHUNKS):
        t0 = c * L
        NKB = TSUB * (c + 1)

        def ldx(e, c=c, t0=t0):
            instr = e.dma_start(out=x_tm[:].rearrange("p (s d) -> p s d", d=D),
                                in_=x_h[t0:t0 + L, :].rearrange("(s p) d -> p s d", p=128))
            kb.inc(instr, "d_x", 16)
            return None

        base = kb.snap("d_x", "dv_st", "sc_ln", "dv_ln", "mm_ln", "dv_ev")

        def ln_ve(e, c=c, base=base):
            e.wait_ge(sem("d_x"), base["d_x"] + 16)
            last = None
            for ts in range(TSUB):
                xt = x_tm[:, ts * D:(ts + 1) * D]
                e.bn_stats(out=scr0[:, 0:6], in_=xt[:, 0:512])
                e.bn_stats(out=scr0[:, 6:12], in_=xt[:, 512:1024])
                e.drain()
                e.bn_aggr(out=scr0[:, 12:14], in_=scr0[:, 0:12])
                e.drain()
                last = e.tensor_scalar(out=scr0[:, 20:21], in0=scr0[:, 13:14],
                                       scalar1=LN_EPS, scalar2=None, op0=ALU.add)
                kb.inc(last, "dv_st", 1)
                e.wait_ge(sem("sc_ln"), base["sc_ln"] + ts + 1)
                if ts >= 1:
                    e.wait_ge(sem("mm_ln"), base["mm_ln"] + 2 * ts)
                e.scalar_tensor_tensor(out=scr0[:, 23:24], in0=scr0[:, 12:13],
                                       scalar=-1.0, in1=scr0[:, 24:25],
                                       op0=ALU.mult, op1=ALU.mult)
                e.drain()
                last = e.tensor_scalar(out=ktm[:, 0:D], in0=xt,
                                       scalar1=scr0[:, 24:25], scalar2=scr0[:, 23:24],
                                       op0=ALU.mult, op1=ALU.add)
                kb.inc(last, "dv_ln", 1)
                for wave in range(2):
                    e.wait_ge(sem("mm_ln"), base["mm_ln"] + 2 * ts + wave + 1)
                    for j in range(4):
                        dc = wave * 4 + j
                        last = e.tensor_copy(
                            out=xn_fm[:, dc * (L + 1) + 1 + ts * 128:
                                      dc * (L + 1) + 1 + (ts + 1) * 128],
                            in_=tslot(dc))
                    kb.inc(last, "dv_ev", 1)
            return last

        def ln_sc(e, c=c, base=base):
            last = None
            for ts in range(TSUB):
                e.wait_ge(sem("dv_st"), base["dv_st"] + ts + 1)
                e.activation(out=scr0[:, 21:22], in_=scr0[:, 20:21], func=AF.Ln)
                last = e.activation(out=scr0[:, 24:25], in_=scr0[:, 21:22],
                                    func=AF.Exp, scale=-0.5)
                kb.inc(last, "sc_ln", 1)
            return last

        def ln_te(e, c=c, base=base):
            last = None
            for ts in range(TSUB):
                e.wait_ge(sem("dv_ln"), base["dv_ln"] + ts + 1)
                for wave in range(2):
                    w = 2 * ts + wave
                    if w >= 1:
                        e.wait_ge(sem("dv_ev"), base["dv_ev"] + w)
                    for j in range(4):
                        dc = wave * 4 + j
                        last = e.matmul(tslot(dc), lhsT=ktm[:, dc * 128:(dc + 1) * 128],
                                        rhs=ident[:], start=True, stop=True)
                    kb.inc(last, "mm_ln", 1)
            return last

        kb.phase(sync=ldx, vector=ln_ve, scalar=ln_sc, tensor=ln_te)
        kb.barrier()
        tap(f"xnfm_c{c}")



        def pair_slot(h):
            if h < 4:
                return h, 0
            if h < 8:
                return h - 4, 64
            if h < 12:
                return 4 + (h - 8), 0
            return (h - 12) + 4, 64

        base = kb.snap("d_w", "mm_tr", "sc_q2", "sc_rsq", "sc_k2",
                       "sc_rsk", "dv_qred", "dv_kred", "dv_qs", "dv_kv", "dv_rope",
                       "dv_tr", "wuseB", "wuseC")
        sem("wuseB")
        sem("wuseC")
        wq_tgt = {}
        wk_tgt = {}

        def qkv_sync(e, c=c, base=base):
            for ts in range(TSUB):
                for ck in range(DC):
                    g = ts * DC + ck
                    if g >= 2:
                        e.wait_ge(sem("wuseB"), base["wuseB"] + 2 * (g - 1))
                    instr = e.dma_start(out=wringB[:, (g % 2) * 1024:(g % 2) * 1024 + 1024],
                                        in_=wc_q[ck, :, :])
                    wq_tgt[(ts, ck)] = (g % 2, kb.inc(instr, f"d_wB{g % 2}", 16))
                for ck in range(DC):
                    g = ts * DC + ck
                    if g >= 2:
                        e.wait_ge(sem("wuseC"), base["wuseC"] + g - 1)
                    instr = e.dma_start(out=wringC[:, (g % 2) * 512:(g % 2) * 512 + 512],
                                        in_=wc_kv[ck, :, :])
                    wk_tgt[(ts, ck)] = (g % 2, kb.inc(instr, f"d_wC{g % 2}", 16))
            return None

        def xn_tile(ck, ts):
            return xn_fm[:, ck * (L + 1) + 1 + ts * 128: ck * (L + 1) + 1 + (ts + 1) * 128]

        def qkv_te(e, c=c, base=base):
            last = None
            for ts in range(TSUB):
                if ts >= 1:
                    e.wait_ge(sem("dv_qs"), base["dv_qs"] + ts)
                for ck in range(DC):
                    g = ts * DC + ck
                    _s, _t = wq_tgt[(ts, ck)]
                    e.wait_ge(sem(f"d_wB{_s}"), _t)
                    for hh in range(2):
                        last = e.matmul(psE[:, hh * 512:(hh + 1) * 512],
                                        lhsT=xn_tile(ck, ts),
                                        rhs=wringB[:, (g % 2) * 1024 + hh * 512:
                                                   (g % 2) * 1024 + (hh + 1) * 512],
                                        start=(ck == 0), stop=(ck == DC - 1))
                        kb.inc(last, "wuseB", 1)
                if ts >= 1:
                    e.wait_ge(sem("dv_kv"), base["dv_kv"] + ts)
                for ck in range(DC):
                    g = ts * DC + ck
                    _s, _t = wk_tgt[(ts, ck)]
                    e.wait_ge(sem(f"d_wC{_s}"), _t)
                    last = e.matmul(psB[:], lhsT=xn_tile(ck, ts),
                                    rhs=wringC[:, (g % 2) * 512:(g % 2) * 512 + 512],
                                    start=(ck == 0), stop=(ck == DC - 1))
                    kb.inc(last, "wuseC", 1)
                e.wait_ge(sem("dv_rope"), base["dv_rope"] + ts + 1)
                for j in range(10):
                    tr_idx = ts * 10 + j
                    if tr_idx >= 4:
                        e.wait_ge(sem("dv_tr"), base["dv_tr"] + tr_idx - 3)
                    src = (ropet[:, j * 128:(j + 1) * 128] if j < 8 else
                           scrK[:, 256 + (j - 8) * 128:256 + (j - 7) * 128])
                    last = e.matmul([psA, psC, psD, psT][tr_idx % 4][:, 0:128], lhsT=src, rhs=ident[:], start=True, stop=True)
                    kb.inc(last, "mm_tr", 1)
            return last

        def qkv_sc(e, c=c, base=base, d_out_t=None):
            last = None
            e.wait_ge(sem("d_out"), C["d_out"])
            for ts in range(TSUB):
                e.wait_ge(sem("wuseB"), base["wuseB"] + 16 * (ts + 1))
                if ts >= 1:
                    e.wait_ge(sem("dv_qred"), base["dv_qred"] + ts)
                e.activation(out=psC[:], in_=psE[:, 0:512], func=AF.Square)
                last = e.activation(out=psD[:], in_=psE[:, 512:1024], func=AF.Square)
                kb.inc(last, "sc_q2", 1)
                e.wait_ge(sem("dv_qred"), base["dv_qred"] + ts + 1)
                e.activation(out=scr0[:, 40:56], in_=scr0[:, 40:56], func=AF.Ln)
                last = e.activation(out=scr0[:, 40:56], in_=scr0[:, 40:56],
                                    func=AF.Exp, scale=-0.5)
                kb.inc(last, "sc_rsq", 1)
                e.wait_ge(sem("wuseC"), base["wuseC"] + DC * (ts + 1))
                last = e.activation(out=scr3[:, 0:256], in_=psB[:, 0:256], func=AF.Square)
                kb.inc(last, "sc_k2", 1)
                e.wait_ge(sem("dv_kred"), base["dv_kred"] + ts + 1)
                e.activation(out=scr0[:, 56:60], in_=scr0[:, 56:60], func=AF.Ln)
                last = e.activation(out=scr0[:, 56:60], in_=scr0[:, 56:60],
                                    func=AF.Exp, scale=-0.5)
                kb.inc(last, "sc_rsk", 1)
            return last

        def qkv_ve(e, c=c, base=base):
            last = None
            for ts in range(TSUB):
                tb = c * TSUB + ts
                cosb = cos_tm[:, tb * 32:(tb + 1) * 32].unsqueeze(1).broadcast_to([128, H, 32])
                sinb = sin_tm[:, tb * 32:(tb + 1) * 32].unsqueeze(1).broadcast_to([128, H, 32])
                cosb4 = cos_tm[:, tb * 32:(tb + 1) * 32].unsqueeze(1).broadcast_to([128, 4, 32])
                sinb4 = sin_tm[:, tb * 32:(tb + 1) * 32].unsqueeze(1).broadcast_to([128, 4, 32])
                e.wait_ge(sem("sc_q2"), base["sc_q2"] + ts + 1)
                e.tensor_reduce(out=scr0[:, 40:48],
                                in_=psC[:].rearrange("p (h d) -> p h d", d=HD),
                                axis=mybir.AxisListType.X, op=ALU.add)
                e.tensor_reduce(out=scr0[:, 48:56],
                                in_=psD[:].rearrange("p (h d) -> p h d", d=HD),
                                axis=mybir.AxisListType.X, op=ALU.add)
                e.drain()
                last = e.tensor_scalar(out=scr0[:, 40:56], in0=scr0[:, 40:56],
                                       scalar1=1.0 / HD, scalar2=RMS_EPS,
                                       op0=ALU.mult, op1=ALU.add)
                kb.inc(last, "dv_qred", 1)
                e.wait_ge(sem("sc_rsq"), base["sc_rsq"] + ts + 1)
                rsb = scr0[:, 40:56].unsqueeze(2).broadcast_to([128, H, HD])
                last = e.tensor_tensor(out=ktm[:, 0:1024].rearrange("p (h d) -> p h d", d=HD),
                                       in0=psE[:].rearrange("p (h d) -> p h d", d=HD),
                                       in1=rsb, op=ALU.mult)
                kb.inc(last, "dv_qs", 1)
                e.drain()
                if ts >= 1:
                    e.wait_ge(sem("mm_tr"), base["mm_tr"] + 10 * ts - 2)
                qs = ktm[:, 0:1024].rearrange("p (h d) -> p h d", d=HD)
                rp = ropet[:].rearrange("p (h d) -> p h d", d=HD)
                tmpA = scrT[:, 0:512].rearrange("p (h d) -> p h d", d=32)
                tmpB = abuf[:, 0:512].rearrange("p (h d) -> p h d", d=32)
                e.tensor_tensor(out=tmpA, in0=qs[:, :, 32:64], in1=sinb, op=ALU.mult)
                e.tensor_tensor(out=tmpB, in0=qs[:, :, 0:32], in1=sinb, op=ALU.mult)
                e.tensor_tensor(out=rp[:, :, 0:32], in0=qs[:, :, 0:32], in1=cosb, op=ALU.mult)
                e.tensor_tensor(out=rp[:, :, 32:64], in0=qs[:, :, 32:64], in1=cosb, op=ALU.mult)
                e.tensor_tensor(out=rp[:, :, 0:32], in0=rp[:, :, 0:32], in1=tmpA, op=ALU.add)
                e.tensor_tensor(out=rp[:, :, 32:64], in0=rp[:, :, 32:64], in1=tmpB,
                                op=ALU.subtract)
                e.wait_ge(sem("sc_k2"), base["sc_k2"] + ts + 1)
                e.tensor_reduce(out=scr0[:, 56:60],
                                in_=scr3[:, 0:256].rearrange("p (h d) -> p h d", d=HD),
                                axis=mybir.AxisListType.X, op=ALU.add)
                e.drain()
                last = e.tensor_scalar(out=scr0[:, 56:60], in0=scr0[:, 56:60],
                                       scalar1=1.0 / HD, scalar2=RMS_EPS,
                                       op0=ALU.mult, op1=ALU.add)
                kb.inc(last, "dv_kred", 1)
                e.wait_ge(sem("sc_rsk"), base["sc_rsk"] + ts + 1)
                rskb = scr0[:, 56:60].unsqueeze(2).broadcast_to([128, 4, HD])
                e.tensor_tensor(out=scrK[:, 0:256].rearrange("p (h d) -> p h d", d=HD),
                                in0=psB[:, 0:256].rearrange("p (h d) -> p h d", d=HD),
                                in1=rskb, op=ALU.mult)
                if ts >= 1:
                    e.wait_ge(sem("mm_tr"), base["mm_tr"] + 10 * ts)
                ks = scrK[:, 0:256].rearrange("p (h d) -> p h d", d=HD)
                kp = scrK[:, 256:512].rearrange("p (h d) -> p h d", d=HD)
                tmp4 = scrT[:, 0:128].rearrange("p (h d) -> p h d", d=32)
                e.tensor_tensor(out=kp[:, :, 0:32], in0=ks[:, :, 0:32], in1=cosb4, op=ALU.mult)
                e.tensor_tensor(out=tmp4, in0=ks[:, :, 32:64], in1=sinb4, op=ALU.mult)
                e.tensor_tensor(out=kp[:, :, 0:32], in0=kp[:, :, 0:32], in1=tmp4, op=ALU.add)
                e.tensor_tensor(out=tmp4, in0=ks[:, :, 0:32], in1=sinb4, op=ALU.mult)
                e.tensor_tensor(out=kp[:, :, 32:64], in0=ks[:, :, 32:64], in1=cosb4, op=ALU.mult)
                last = e.tensor_tensor(out=kp[:, :, 32:64], in0=kp[:, :, 32:64], in1=tmp4,
                                       op=ALU.subtract)
                kb.inc(last, "dv_rope", 1)
                va = vglob[:].rearrange("p (s o) -> p s o", o=65)
                for kvh in range(KVH):
                    last = e.tensor_copy(out=va[:, kvh * (S // 128) + tb, 0:64],
                                         in_=psB[:, 256 + kvh * 64:256 + (kvh + 1) * 64])
                kb.inc(last, "dv_kv", 1)
                for j in range(10):
                    tr_idx = ts * 10 + j
                    e.wait_ge(sem("mm_tr"), base["mm_tr"] + tr_idx + 1)
                    slot = [psA, psC, psD, psT][tr_idx % 4][:, 0:128]
                    if j < 8:
                        for jj in range(2):
                            h = 2 * j + jj
                            p_idx, row0 = pair_slot(h)
                            last = e.tensor_copy(
                                out=qhat[row0:row0 + 64,
                                         p_idx * L + ts * 128:p_idx * L + (ts + 1) * 128],
                                in_=slot[jj * 64:(jj + 1) * 64, :])
                    else:
                        kchunk = j - 8
                        cslice = kglob[:, kchunk * S + c * L + ts * 128:
                                       kchunk * S + c * L + (ts + 1) * 128]
                        last = e.tensor_copy(out=cslice, in_=slot[:])
                    kb.inc(last, "dv_tr", 1)
            return last

        kb.phase(sync=qkv_sync, tensor=qkv_te, scalar=qkv_sc, vector=qkv_ve)
        kb.barrier()
        tap(f"qhat_c{c}")
        tap(f"kglob_c{c}")
        tap(f"vglob_c{c}")

        base = kb.snap("mm_s", "sc_exp", "dv_mask", "mm_pv", "dv_den", "mm_rbc",
                       "dv_yev", "d_df")
        expcnt = {"n": 0}
        pvcnt = {"n": 0}

        def att_te(e, c=c, base=base, NKB=NKB):
            last = None
            for p in range(DC):
                hA, hB = (p, p + 4) if p < 4 else (p + 4, p + 8)
                kchunk = p // 4
                if p >= 1:
                    e.wait_ge(sem("dv_yev"), base["dv_yev"] + p)
                for kbi in range(NKB):
                    pe = [psA, psC][kbi % 2]
                    po = [psB, psD][kbi % 2]
                    idx = p * NKB + kbi
                    if kbi >= 2:
                        e.wait_ge(sem("sc_exp"), base["sc_exp"] + 2 * (p * NKB + kbi - 1))
                    kcols = slice(kchunk * S + kbi * 128, kchunk * S + (kbi + 1) * 128)
                    last = e.matmul(pe[:], lhsT=kglob[0:64, kcols],
                                    rhs=qhat[0:64, p * L:(p + 1) * L],
                                    start=True, stop=True, tile_position=(0, 0))
                    last = e.matmul(po[:], lhsT=kglob[64:128, kcols],
                                    rhs=qhat[64:128, p * L:(p + 1) * L],
                                    start=True, stop=True, tile_position=(64, 0))
                    kb.inc(last, "mm_s", 2)
                    for jj, h in ((0, hA), (1, hB)):
                        eslot = (2 * (p * NKB + kbi) + jj) % 4
                        if kbi >= TSUB * c:
                            e.wait_ge(sem("dv_mask"), msk_target(p, kbi, jj, c, NKB))
                        else:
                            e.wait_ge(sem("sc_exp"), base["sc_exp"] + 2 * (p * NKB + kbi) + jj + 1)
                        kvh = h // 4
                        va = vglob[:].rearrange("p (s o) -> p s o", o=65)
                        vs = va[:, kvh * (S // 128) + kbi, :]
                        py = psY0 if jj == 0 else psY1
                        last = e.matmul(py, lhsT=vs, rhs=expS[:, eslot * L:(eslot + 1) * L],
                                        start=(kbi == 0), stop=(kbi == NKB - 1))
                        kb.inc(last, "mm_pv", 1)
            return last

        def msk_target(p, kbi, jj, c, NKB):
            per_pair = 2 * TSUB
            done_pairs = p * per_pair
            done_this = 2 * (kbi - TSUB * c) + jj + 1
            return base_attn_mask + done_pairs + done_this

        def att_sc(e, c=c, base=base, NKB=NKB):
            last = None
            for p in range(DC):
                hA, hB = (p, p + 4) if p < 4 else (p + 4, p + 8)
                for kbi in range(NKB):
                    pe = [psA, psC][kbi % 2]
                    po = [psB, psD][kbi % 2]
                    e.wait_ge(sem("mm_s"), base["mm_s"] + 2 * (p * NKB + kbi) + 2)
                    for jj, h, pp in ((0, hA, pe), (1, hB, po)):
                        eslot = (2 * (p * NKB + kbi) + jj) % 4
                        idx = 2 * (p * NKB + kbi) + jj
                        if idx >= 4:
                            e.wait_ge(sem("mm_pv"), base["mm_pv"] + idx - 3)
                        last = e.activation(out=expS[:, eslot * L:(eslot + 1) * L],
                                            in_=pp[:], func=AF.Exp,
                                            bias=qbias[:, h:h + 1],
                                            scale=qscale[:, h:h + 1])
                        kb.inc(last, "sc_exp", 1)
            return last

        def att_ve(e, c=c, base=base, NKB=NKB):
            last = None
            for p in range(DC):
                hA, hB = (p, p + 4) if p < 4 else (p + 4, p + 8)
                for kbi in range(TSUB * c, NKB):
                    j = kbi - TSUB * c
                    for jj in range(2):
                        eslot = (2 * (p * NKB + kbi) + jj) % 4
                        e.wait_ge(sem("sc_exp"), base["sc_exp"] + 2 * (p * NKB + kbi) + jj + 1)
                        last = e.tensor_tensor(out=expS[:, eslot * L:(eslot + 1) * L],
                                               in0=expS[:, eslot * L:(eslot + 1) * L],
                                               in1=ramp[:, 384 - 128 * j:384 - 128 * j + L],
                                               op=ALU.mult)
                        kb.inc(last, "dv_mask", 1)
                e.wait_ge(sem("mm_pv"), base["mm_pv"] + 2 * (p + 1) * NKB)
                if p >= 1:
                    e.wait_ge(sem("d_rbc"), base["d_rbc"] + 32 * p)
                e.tensor_copy(out=den2A, in_=psY0[64:65, :])
                e.tensor_copy(out=den2B, in_=psY1[64:65, :])
                e.reciprocal(out=den2A, in_=den2A)
                e.reciprocal(out=den2B, in_=den2B)
                e.tensor_scalar(out=den2A, in0=den2A, scalar1=alcols[64:65, 0:1],
                                scalar2=None, op0=ALU.mult)
                last = e.tensor_scalar(out=den2B, in0=den2B, scalar1=alcols[64:65, 0:1],
                                       scalar2=None, op0=ALU.mult)
                kb.inc(last, "dv_den", 2)
                e.wait_ge(sem("d_rbc"), base["d_rbc"] + 32 * p + 32)
                dcA, halfA = hA // 2, (hA % 2) * 64
                dcB, halfB = hB // 2, (hB % 2) * 64
                tmpdsts = []
                for jj, h, py, rb, dc_, half in ((0, hA, psY0, rbcA, dcA, halfA),
                                                 (1, hB, psY1, rbcB, dcB, halfB)):
                    if half == 0:
                        last = e.tensor_tensor(
                            out=attn_s[0:64, dc_ * L:(dc_ + 1) * L],
                            in0=py[0:64, :], in1=rb, op=ALU.mult)
                    else:
                        dst = scrT[0:64, 0:L] if jj == 0 else bbuf[0:64, 0:L]
                        e.tensor_tensor(out=dst, in0=py[0:64, :], in1=rb, op=ALU.mult)
                        tmpdsts.append((dst, dc_))
                for dst, dc_ in tmpdsts:
                    last = e.tensor_copy(out=attn_s[64:128, dc_ * L:(dc_ + 1) * L],
                                         in_=dst)
                kb.inc(last, "dv_yev", 1)
            return last

        def att_gp(e, c=c, base=base):
            for p in range(DC):
                for jj in range(2):
                    idx = 2 * p + jj
                    e.wait_ge(sem("dv_den"), base["dv_den"] + 2 * p + 2)
                    if p >= 1 and jj == 0:
                        e.wait_ge(sem("dv_yev"), base["dv_yev"] + p)
                    dd = den2A if jj == 0 else den2B
                    e.dma_start(out=den_hbm, in_=dd).then_inc(sem("d_rbcW"), 16)
                    C["d_rbcW"] = C.get("d_rbcW", 0) + 16
                    e.wait_ge(sem("d_rbcW"), C["d_rbcW"])
                    rb = rbcA if jj == 0 else rbcB
                    instr = e.dma_start(out=rb, in_=bass.AP(den_hbm.tensor, 0,
                                                            [[0, 64], [1, L]]))
                    instr.then_inc(sem("d_rbc"), 16)
                    C["d_rbc"] = C.get("d_rbc", 0) + 16
            return None

        base_attn_mask = C.get("dv_mask", 0)
        sem("d_rbc")
        sem("d_rbcW")
        base["d_rbc"] = C.get("d_rbc", 0)
        kb.phase(tensor=att_te, scalar=att_sc, vector=att_ve, gpsimd=att_gp)
        kb.barrier()
        tap(f"attn_c{c}")

        base = kb.snap("wuse", "dv_xs", "mm_cv", "sc_silu", "mm_par", "dv_gt")

        def s1_sync(e, c=c):
            for ic in range(NIC):
                for ck in range(DC):
                    wload(e, wc_mamba[ic, ck, :, :])
                for ck in range(DC):
                    wload(e, wc_mamba[NIC + ic, ck, :, :])
            return None

        def s1_te(e, c=c, base=base):
            last = None
            for ic in range(NIC):
                pp = psA if ic % 2 == 0 else psC
                pg = psB if ic % 2 == 0 else psD
                if ic >= 2:
                    e.wait_ge(sem("dv_xs"), base["dv_xs"] + ic - 1)
                    e.wait_ge(sem("dv_gt"), base["dv_gt"] + ic - 1)
                for ck in range(DC):
                    last = wmm(e, pp[:], xn_ck(ck), ck == 0, ck == DC - 1)
                for ck in range(DC):
                    last = wmm(e, pg[:], xn_ck(ck), ck == 0, ck == DC - 1)
                e.wait_ge(sem("dv_xs"), base["dv_xs"] + ic + 1)
                if ic >= 1:
                    e.wait_ge(sem("sc_silu"), base["sc_silu"] + ic)
                for tau in range(KC):
                    last = e.matmul(psT[:], lhsT=bbuf[:, tau * 128:(tau + 1) * 128],
                                    rhs=pn.tensor.ap()[:, tau:tau + L],
                                    start=(tau == 0), stop=(tau == KC - 1))
                kb.inc(last, "mm_cv", 1)
                e.wait_ge(sem("sc_silu"), base["sc_silu"] + ic + 1)
                last = e.matmul(psE[0:96, 0:L], lhsT=wx_T[:, ic * 96:(ic + 1) * 96],
                                rhs=u_g[:, ic * L:(ic + 1) * L],
                                start=(ic == 0), stop=(ic == NIC - 1))
                kb.inc(last, "mm_par", 1)
            return last

        def s1_ve(e, c=c, base=base):
            last = None
            for ic in range(NIC):
                pp = psA if ic % 2 == 0 else psC
                pg = psB if ic % 2 == 0 else psD
                if ic >= 1:
                    e.wait_ge(sem("mm_cv"), base["mm_cv"] + ic)
                for tau in range(KC):
                    e.tensor_scalar(out=bbuf[:, tau * 128:(tau + 1) * 128], in0=ident[:],
                                    scalar1=cwcols[:, ic * KC + tau:ic * KC + tau + 1],
                                    scalar2=None, op0=ALU.mult)
                e.wait_ge(sem("wuse"), base["wuse"] + 16 * ic + 8)
                e.tensor_copy(out=pn.tensor.ap()[:, 0:3], in_=convhalo[:, ic * 3:(ic + 1) * 3])
                e.tensor_copy(out=pn.tensor.ap()[:, 3:3 + L], in_=pp[:])
                e.drain()
                last = e.tensor_copy(out=convhalo[:, ic * 3:(ic + 1) * 3],
                                     in_=pn.tensor.ap()[:, L:L + 3])
                kb.inc(last, "dv_xs", 1)
                e.wait_ge(sem("wuse"), base["wuse"] + 16 * ic + 16)
                last = e.tensor_copy(out=gate_bf[:, ic * L:(ic + 1) * L], in_=pg[:])
                kb.inc(last, "dv_gt", 1)
            return last

        def s1_sc(e, c=c, base=base):
            last = None
            for ic in range(NIC):
                e.wait_ge(sem("mm_cv"), base["mm_cv"] + ic + 1)
                last = e.activation(out=u_g[:, ic * L:(ic + 1) * L], in_=psT[:],
                                    func=AF.Silu, bias=cbcols[:, ic:ic + 1])
                kb.inc(last, "sc_silu", 1)
            return last

        kb.phase(sync=s1_sync, tensor=s1_te, vector=s1_ve, scalar=s1_sc)
        kb.barrier()
        tap(f"u_c{c}")
        tap(f"gate_c{c}")

        base = kb.snap("dv_par", "d_bf", "mm_bc", "dv_bc")
        bc_tgt = {}

        def par_ve(e, c=c, base=base):
            e.tensor_copy(out=dtlow[0:64, :], in_=psE[0:64, 0:L])
            e.tensor_copy(out=dtlow[64:128, :], in_=psE[0:64, 0:L])
            last = e.tensor_copy(out=scrB[:], in_=psE[64:96, 0:L])
            kb.inc(last, "dv_par", 1)
            last2 = None
            for n2 in range(2 * N):
                e.wait_ge(sem("mm_bc"), base["mm_bc"] + n2 + 1)
                pp = psA if n2 % 2 == 0 else psB
                dst = bbc if n2 < N else cbc
                nn = n2 if n2 < N else n2 - N
                last2 = e.tensor_copy(out=dst[:, nn * L:(nn + 1) * L], in_=pp[:])
                kb.inc(last2, "dv_bc", 1)
            return last2

        def par_gp(e, c=c, base=base):
            e.wait_ge(sem("dv_par"), base["dv_par"] + 1)
            for r in range(2 * N):
                if r >= 2:
                    e.wait_ge(sem("mm_bc"), base["mm_bc"] + r - 1)
                instr = e.dma_start(out=scrBF[0:1, (r % 2) * L:(r % 2) * L + L],
                                    in_=scrB[r:r + 1, :])
                bc_tgt[r] = (r % 2, kb.inc(instr, f"d_bf{r % 2}", 16))
            return None

        def par_te(e, c=c, base=base):
            last = None
            for n2 in range(2 * N):
                _s, _t = bc_tgt[n2]
                e.wait_ge(sem(f"d_bf{_s}"), _t)
                if n2 >= 2:
                    e.wait_ge(sem("dv_bc"), base["dv_bc"] + n2 - 1)
                pp = psA if n2 % 2 == 0 else psB
                last = e.matmul(pp[:], lhsT=onesbf[0:1, :],
                                rhs=scrBF[0:1, (n2 % 2) * L:(n2 % 2) * L + L],
                                start=True, stop=True)
                kb.inc(last, "mm_bc", 1)
            return last

        kb.phase(vector=par_ve, gpsimd=par_gp, tensor=par_te)
        kb.barrier()
        tap(f"bbc_c{c}")
        tap(f"cbc_c{c}")
        tap(f"dtlow_c{c}")

        base = kb.snap("mm_dt", "dv_sp", "sc_sp", "dv_delta", "sc_a", "dv_scan",
                       "dv_p", "mm_y", "dv_yg", "sc_sg")

        def s2_te(e, c=c, base=base):
            last = None
            for ic in range(NIC):
                pp = psA if ic % 2 == 0 else psC
                py = psB if ic % 2 == 0 else psD
                if ic >= 2:
                    e.wait_ge(sem("dv_delta"), base["dv_delta"] + ic - 1)
                r0 = (ic % 2) * 64
                last = e.matmul(pp[:], lhsT=wdt_T[r0:r0 + 64, (ic // 2) * 128:(ic // 2 + 1) * 128],
                                rhs=dtlow[r0:r0 + 64, :], start=True, stop=True)
                kb.inc(last, "mm_dt", 1)
                if ic >= 2:
                    e.wait_ge(sem("dv_yg"), base["dv_yg"] + ic - 1)
                for n in range(N):
                    k = ic * N + n
                    e.wait_ge(sem("dv_p"), base["dv_p"] + k + 1)
                    last = e.matmul(py[:], lhsT=ident[:], rhs=pn[:, (k % 2) * L:(k % 2) * L + L],
                                    start=(n == 0), stop=(n == N - 1))
                    kb.inc(last, "mm_y", 1)
            return last

        def s2_ve(e, c=c, base=base):
            last = None
            for ic in range(NIC):
                pp = psA if ic % 2 == 0 else psC
                py = psB if ic % 2 == 0 else psD
                e.wait_ge(sem("mm_dt"), base["mm_dt"] + ic + 1)
                last = e.tensor_scalar(out=scr2[:], in0=pp[:], scalar1=bdtcols[:, ic:ic + 1],
                                       scalar2=30.0, op0=ALU.add, op1=ALU.min)
                kb.inc(last, "dv_sp", 1)
                e.wait_ge(sem("sc_sp"), base["sc_sp"] + ic + 1)
                e.scalar_tensor_tensor(out=delta_sb[:], in0=pp[:],
                                       scalar=bdtcols[:, ic:ic + 1], in1=scr2[:],
                                       op0=ALU.add, op1=ALU.max)
                e.drain()
                last = e.tensor_tensor(out=du_bf[:], in0=delta_sb[:],
                                       in1=u_g[:, ic * L:(ic + 1) * L], op=ALU.mult)
                e.drain()
                kb.inc(last, "dv_delta", 1)
                e.tensor_tensor(out=bbuf[:, 0:L], in0=du_bf[:],
                                in1=bbc[:, 0:L], op=ALU.mult)
                e.drain()
                for n in range(N):
                    k = ic * N + n
                    e.wait_ge(sem("sc_a"), base["sc_a"] + k + 1)
                    last = e.tensor_tensor_scan(
                        out=hbuf[:], data0=abuf[:, (k % 2) * L:(k % 2) * L + L],
                        data1=bbuf[:, (n % 2) * L:(n % 2) * L + L],
                        initial=state[:, ic * N + n:ic * N + n + 1],
                        op0=ALU.mult, op1=ALU.add)
                    kb.inc(last, "dv_scan", 1)
                    if n + 1 < N:
                        e.tensor_tensor(out=bbuf[:, ((n + 1) % 2) * L:((n + 1) % 2) * L + L],
                                        in0=du_bf[:], in1=bbc[:, (n + 1) * L:(n + 2) * L],
                                        op=ALU.mult)
                    if k >= 2:
                        e.wait_ge(sem("mm_y"), base["mm_y"] + k - 1)
                    last = e.tensor_tensor(out=pn[:, (k % 2) * L:(k % 2) * L + L],
                                           in0=hbuf[:], in1=cbc[:, n * L:(n + 1) * L],
                                           op=ALU.mult)
                    kb.inc(last, "dv_p", 1)
                    e.tensor_copy(out=state[:, ic * N + n:ic * N + n + 1],
                                  in_=hbuf[:, L - 1:L])
                e.wait_ge(sem("mm_y"), base["mm_y"] + (ic + 1) * N)
                e.scalar_tensor_tensor(out=scr2[:], in0=u_g[:, ic * L:(ic + 1) * L],
                                       scalar=dcols[:, ic:ic + 1], in1=py[:],
                                       op0=ALU.mult, op1=ALU.add)
                e.wait_ge(sem("sc_sg"), base["sc_sg"] + ic + 1)
                e.drain()
                last = e.tensor_tensor(out=y_gated[:, ic * L:(ic + 1) * L], in0=scr2[:],
                                       in1=sg_sb[:], op=ALU.mult)
                kb.inc(last, "dv_yg", 1)
            return last

        def s2_sc(e, c=c, base=base):
            last = None
            for ic in range(NIC):
                e.wait_ge(sem("dv_sp"), base["dv_sp"] + ic + 1)
                e.activation(out=scr3[:], in_=scr2[:], func=AF.Exp)
                last = e.activation(out=scr2[:], in_=scr3[:], func=AF.Ln, bias=1.0)
                kb.inc(last, "sc_sp", 1)
                e.wait_ge(sem("dv_delta"), base["dv_delta"] + ic + 1)
                for n in range(N):
                    k = ic * N + n
                    if k >= 2:
                        e.wait_ge(sem("dv_scan"), base["dv_scan"] + k - 1)
                    last = e.activation(out=abuf[:, (k % 2) * L:(k % 2) * L + L],
                                        in_=delta_sb[:], func=AF.Exp,
                                        scale=acols[:, ic * N + n:ic * N + n + 1])
                    kb.inc(last, "sc_a", 1)
                if ic >= 1:
                    e.wait_ge(sem("dv_yg"), base["dv_yg"] + ic)
                last = e.activation(out=sg_sb[:], in_=gate_bf[:, ic * L:(ic + 1) * L],
                                    func=AF.Silu)
                kb.inc(last, "sc_sg", 1)
            return last

        kb.phase(tensor=s2_te, vector=s2_ve, scalar=s2_sc)
        kb.barrier()
        tap(f"ygated_c{c}")

        base = kb.snap("wuse", "dv_mg")

        def mo_sync(e, c=c):
            for dc in range(DC):
                for ic in range(NIC):
                    wload(e, wc_mo[dc, ic, :, :])
            return None

        def mo_te(e, c=c, base=base):
            last = None
            for dc in range(DC):
                pp = psA if dc % 2 == 0 else psB
                if dc >= 2:
                    e.wait_ge(sem("dv_mg"), base["dv_mg"] + dc - 1)
                for ic in range(NIC):
                    last = wmm(e, pp[:], y_gated[:, ic * L:(ic + 1) * L],
                               ic == 0, ic == NIC - 1)
            return last

        def mo_ve(e, c=c, base=base):
            last = None
            for dc in range(DC):
                pp = psA if dc % 2 == 0 else psB
                e.wait_ge(sem("wuse"), base["wuse"] + NIC * (dc + 1))
                last = e.scalar_tensor_tensor(out=merged[:, dc * L:(dc + 1) * L],
                                              in0=pp[:], scalar=alcols[:, 1:2],
                                              in1=attn_s[:, dc * L:(dc + 1) * L],
                                              op0=ALU.mult, op1=ALU.add)
                kb.inc(last, "dv_mg", 1)
            return last

        kb.phase(sync=mo_sync, tensor=mo_te, vector=mo_ve)
        kb.barrier()
        tap(f"merged_c{c}")

        base = kb.snap("dv_xrk", "wuse", "dv_kact")

        def fk_sync(e, c=c):
            for dc in range(DC):
                for ck in range(DC):
                    wload(e, wc_fk[dc, ck, :, :])
            return None

        def fk_ve(e, c=c, base=base):
            last = None
            for dc in range(DC + 1):
                if dc < DC:
                    xs = xn_fm[:, dc * (L + 1) + 1:(dc + 1) * (L + 1)]
                    xp = xn_fm[:, dc * (L + 1):(dc + 1) * (L + 1) - 1]
                    e.tensor_tensor(out=scrT[:, (dc % 2) * 0:L] if False else
                                    (scrT[:, 0:L] if dc % 2 == 0 else abuf[:, 0:L]),
                                    in0=xs, in1=xp, op=ALU.subtract)
                if dc >= 1:
                    d0 = dc - 1
                    sl = scrT[:, 0:L] if d0 % 2 == 0 else abuf[:, 0:L]
                    xp0 = xn_fm[:, d0 * (L + 1):(d0 + 1) * (L + 1) - 1]
                    e.scalar_tensor_tensor(out=xr_g[:, d0 * L:(d0 + 1) * L], in0=sl,
                                           scalar=murcols[:, d0:d0 + 1], in1=xp0,
                                           op0=ALU.mult, op1=ALU.add)
                    last = e.scalar_tensor_tensor(out=xk_g[:, d0 * L:(d0 + 1) * L],
                                                  in0=sl,
                                                  scalar=mukcols[:, d0:d0 + 1], in1=xp0,
                                                  op0=ALU.mult, op1=ALU.add)
            kb.inc(last, "dv_xrk", 1)
            for dc in range(DC + 1):
                if dc < DC:
                    pp = psC if dc % 2 == 0 else psD
                    e.wait_ge(sem("wuse"), base["wuse"] + DC * (dc + 1))
                    e.tensor_scalar(out=scrT[:, 0:L] if dc % 2 == 0 else abuf[:, 0:L],
                                    in0=pp[:], scalar1=0.0, scalar2=None, op0=ALU.max)
                if dc >= 1:
                    d0 = dc - 1
                    sl = scrT[:, 0:L] if d0 % 2 == 0 else abuf[:, 0:L]
                    last = e.tensor_tensor(out=kact[:, d0 * L:(d0 + 1) * L], in0=sl,
                                           in1=sl, op=ALU.mult)
                    kb.inc(last, "dv_kact", 1)
            for dc in range(DC):
                last = e.tensor_copy(out=xn_fm[:, dc * (L + 1):dc * (L + 1) + 1],
                                     in_=xn_fm[:, dc * (L + 1) + L:dc * (L + 1) + L + 1])
            return last

        def fk_te(e, c=c, base=base):
            last = None
            e.wait_ge(sem("dv_xrk"), base["dv_xrk"] + 1)
            for dc in range(DC):
                pp = psC if dc % 2 == 0 else psD
                if dc >= 2:
                    e.wait_ge(sem("dv_kact"), base["dv_kact"] + dc - 1)
                for ck in range(DC):
                    last = wmm(e, pp[:], xk_g[:, ck * L:(ck + 1) * L], ck == 0, ck == DC - 1)
            return last

        kb.phase(sync=fk_sync, vector=fk_ve, tensor=fk_te)
        kb.barrier()
        tap(f"kact_c{c}")

        base = kb.snap("sc_r", "dv_ffn", "dv_out", "d_wD", "wuseD", "d_out")
        sem("wuseD")
        sem("d_wD")
        wD_tgt = {}

        def o_sync(e, c=c, t0=t0, base=base):
            g = 0
            for tp in range(2):
                for wi, wcx in ((0, wc_fr), (1, wc_fv), (2, wc_proj)):
                    for ck in range(DC):
                        if g >= 2:
                            e.wait_ge(sem("wuseD"), base["wuseD"] + 4 * (g - 1))
                        instr = e.dma_start(out=wringB[:, (g % 2) * 1024:(g % 2) * 1024 + 1024],
                                            in_=wcx[ck, :, :])
                        wD_tgt[(tp, wi, ck)] = (g % 2, kb.inc(instr, f"d_wD{g % 2}", 16))
                        g += 1
            e.wait_ge(sem("dv_out"), base["dv_out"] + TSUB)
            instr = e.dma_start(out=out_h[t0:t0 + L, :].rearrange("(s p) d -> p s d", p=128),
                                in_=x_tm[:].rearrange("p (s d) -> p s d", d=D))
            kb.inc(instr, "d_out", 16)
            return None

        def xr_tile(ck, ts):
            return xr_g[:, ck * L + ts * 128:ck * L + (ts + 1) * 128]

        def o_te(e, c=c, base=base):
            last = None
            for tp in range(2):
                ts0, ts1 = 2 * tp, 2 * tp + 1
                if tp >= 1:
                    e.wait_ge(sem("dv_out"), base["dv_out"] + 2 * tp)
                for ck in range(DC):
                    _s, _t = wD_tgt[(tp, 0, ck)]
                    e.wait_ge(sem(f"d_wD{_s}"), _t)
                    for hh in range(2):
                        i1_ = e.matmul(psE[:, hh * 512:(hh + 1) * 512], lhsT=xr_tile(ck, ts0),
                                       rhs=wringB[:, (3 * tp * DC + ck) % 2 * 1024 + hh * 512:
                                                  (3 * tp * DC + ck) % 2 * 1024 + (hh + 1) * 512],
                                       start=(ck == 0), stop=(ck == DC - 1))
                        kb.inc(i1_, "wuseD", 1)
                        last = e.matmul([psA, psB][hh][:], lhsT=xr_tile(ck, ts1),
                                        rhs=wringB[:, (3 * tp * DC + ck) % 2 * 1024 + hh * 512:
                                                   (3 * tp * DC + ck) % 2 * 1024 + (hh + 1) * 512],
                                        start=(ck == 0), stop=(ck == DC - 1))
                        kb.inc(last, "wuseD", 1)
                e.wait_ge(sem("sc_r"), base["sc_r"] + tp + 1)
                for ck in range(DC):
                    gk = (3 * tp + 1) * DC + ck
                    _s, _t = wD_tgt[(tp, 1, ck)]
                    e.wait_ge(sem(f"d_wD{_s}"), _t)
                    kt = kact[:, ck * L + ts0 * 128:ck * L + (ts0 + 1) * 128]
                    kt1 = kact[:, ck * L + ts1 * 128:ck * L + (ts1 + 1) * 128]
                    for hh in range(2):
                        i1_ = e.matmul(psE[:, hh * 512:(hh + 1) * 512], lhsT=kt,
                                       rhs=wringB[:, gk % 2 * 1024 + hh * 512:
                                                  gk % 2 * 1024 + (hh + 1) * 512],
                                       start=(ck == 0), stop=(ck == DC - 1))
                        kb.inc(i1_, "wuseD", 1)
                        last = e.matmul([psA, psB][hh][:], lhsT=kt1,
                                        rhs=wringB[:, gk % 2 * 1024 + hh * 512:
                                                   gk % 2 * 1024 + (hh + 1) * 512],
                                        start=(ck == 0), stop=(ck == DC - 1))
                        kb.inc(last, "wuseD", 1)
                e.wait_ge(sem("dv_ffn"), base["dv_ffn"] + 2 * tp + 1)
                for ck in range(DC):
                    gk = (3 * tp + 2) * DC + ck
                    _s, _t = wD_tgt[(tp, 2, ck)]
                    e.wait_ge(sem(f"d_wD{_s}"), _t)
                    mt = merged[:, ck * L + ts0 * 128:ck * L + (ts0 + 1) * 128]
                    mt1 = merged[:, ck * L + ts1 * 128:ck * L + (ts1 + 1) * 128]
                    for hh in range(2):
                        i1_ = e.matmul(psE[:, hh * 512:(hh + 1) * 512], lhsT=mt,
                                       rhs=wringB[:, gk % 2 * 1024 + hh * 512:
                                                  gk % 2 * 1024 + (hh + 1) * 512],
                                       start=(ck == 0), stop=(ck == DC - 1))
                        kb.inc(i1_, "wuseD", 1)
                        last = e.matmul([psA, psB][hh][:], lhsT=mt1,
                                        rhs=wringB[:, gk % 2 * 1024 + hh * 512:
                                                   gk % 2 * 1024 + (hh + 1) * 512],
                                        start=(ck == 0), stop=(ck == DC - 1))
                        kb.inc(last, "wuseD", 1)
            return last

        def o_sc(e, c=c, base=base):
            last = None
            for tp in range(2):
                ts0, ts1 = 2 * tp, 2 * tp + 1
                e.wait_ge(sem("wuseD"), base["wuseD"] + 96 * tp + 32)
                e.activation(out=u_g[:, 4096 + ts0 * 1024:4096 + (ts0 + 1) * 1024],
                             in_=psE[:], func=AF.Sigmoid)
                e.activation(out=u_g[:, 4096 + ts1 * 1024:4096 + ts1 * 1024 + 512],
                             in_=psA[:], func=AF.Sigmoid)
                last = e.activation(out=u_g[:, 4096 + ts1 * 1024 + 512:4096 + (ts1 + 1) * 1024],
                                    in_=psB[:], func=AF.Sigmoid)
                kb.inc(last, "sc_r", 1)
            return last

        def o_ve(e, c=c, base=base):
            last = None
            fscr = u_g.bitcast(F32)
            for tp in range(2):
                ts0, ts1 = 2 * tp, 2 * tp + 1
                e.wait_ge(sem("wuseD"), base["wuseD"] + 96 * tp + 64)
                e.tensor_tensor(out=fscr[:, 0:1024], in0=psE[:],
                                in1=u_g[:, 4096 + ts0 * 1024:4096 + (ts0 + 1) * 1024],
                                op=ALU.mult)
                e.tensor_tensor(out=fscr[:, 1024:1536], in0=psA[:],
                                in1=u_g[:, 4096 + ts1 * 1024:4096 + ts1 * 1024 + 512],
                                op=ALU.mult)
                last = e.tensor_tensor(out=fscr[:, 1536:2048], in0=psB[:],
                                       in1=u_g[:, 4096 + ts1 * 1024 + 512:4096 + (ts1 + 1) * 1024],
                                       op=ALU.mult)
                kb.inc(last, "dv_ffn", 1)
                kb.inc(e.engine_nop(), "dv_ffn", 1)
                e.wait_ge(sem("wuseD"), base["wuseD"] + 96 * tp + 96)
                e.tensor_tensor(out=fscr[:, 0:1024], in0=fscr[:, 0:1024], in1=psE[:],
                                op=ALU.add)
                e.tensor_tensor(out=fscr[:, 1024:1536], in0=fscr[:, 1024:1536], in1=psA[:],
                                op=ALU.add)
                e.tensor_tensor(out=fscr[:, 1536:2048], in0=fscr[:, 1536:2048], in1=psB[:],
                                op=ALU.add)
                e.tensor_tensor(out=x_tm[:, ts0 * D:(ts0 + 1) * D],
                                in0=x_tm[:, ts0 * D:(ts0 + 1) * D],
                                in1=fscr[:, 0:1024], op=ALU.add)
                kb.inc(e.engine_nop(), "dv_out", 1)
                last = e.tensor_tensor(out=x_tm[:, ts1 * D:(ts1 + 1) * D],
                                       in0=x_tm[:, ts1 * D:(ts1 + 1) * D],
                                       in1=fscr[:, 1024:2048], op=ALU.add)
                kb.inc(last, "dv_out", 1)
            return last

        kb.phase(sync=o_sync, tensor=o_te, scalar=o_sc, vector=o_ve)
        kb.barrier()

    def fin_sync(e):
        e.wait_ge(sem("d_out"), C["d_out"])
        return None

    kb.phase(sync=fin_sync)
    return nc


_CACHE = {}


def kernel(**inputs):
    key = "prog"
    if key not in _CACHE:
        _CACHE[key] = build_program()
    nc = _CACHE[key]
    x = np.ascontiguousarray(inputs["L0_attn_x"])
    names = {
        "W_mamba": "L0_attn_W_mamba", "W_q": "L0_attn_W_q", "W_kv": "L0_attn_W_kv",
        "q_gain": "L0_attn_q_gain", "conv_w": "L0_attn_conv_w", "conv_b": "L0_attn_conv_b",
        "W_xproj": "L0_attn_W_xproj", "W_dt": "L0_attn_W_dt", "b_dt": "L0_attn_b_dt",
        "A_log": "L0_attn_A_log", "D": "L0_attn_D", "W_mamba_out": "L0_attn_W_mamba_out",
        "W_proj": "L0_attn_W_proj", "merge_alpha": "L0_attn_merge_alpha",
        "W_r": "L0_ffn_W_r", "W_k": "L0_ffn_W_k", "W_v": "L0_ffn_W_v",
        "mu_r": "L0_ffn_mu_r", "mu_k": "L0_ffn_mu_k",
    }
    shared = {k: np.ascontiguousarray(inputs[v]) for k, v in names.items()}
    in_maps = []
    for b in range(8):
        m = {"x": np.ascontiguousarray(x[b])}
        m.update(shared)
        in_maps.append(m)
    res = run_bass_kernel_spmd(nc, in_maps, core_ids=list(range(8)))
    return np.stack([r["out"] for r in res.results], axis=0)
